# version 06 (final)
# baseline (speedup 1.0000x reference)
import numpy as np
import ml_dtypes
from contextlib import ExitStack

import concourse.bass as bass
import concourse.mybir as mybir
from concourse.bass_utils import run_bass_kernel_spmd
from concourse.alu_op_type import AluOpType

F32 = mybir.dt.float32
BF16 = mybir.dt.bfloat16
F16 = mybir.dt.float16
I16 = mybir.dt.int16
U16 = mybir.dt.uint16
AF = mybir.ActivationFunctionType
OP = AluOpType
AXX = mybir.AxisListType.X

B, S, H, NH, T, NCH = 4, 2048, 2048, 16, 1024, 16
SCALE = float(1.0 / np.sqrt(np.float32(128.0)))
LN_EPS, RMS_EPS = 1e-5, 1e-6
GU_O, GU_J16, DN_O, DN_J16 = 11264, 512, 2048, 704 * 2
GU_KC, DN_KC = 16, 44
G_SBS = [(0, 2048), (2048, 2048), (4096, 2048), (6144, 2048), (8192, 2048),
         (10240, 1024)]
W_G = 2048
ENGS = ("sync", "gpsimd", "tensor", "scalar", "vector")

SEM_NAMES = ["ld_a", "ld_b", "ld_c", "ld_d", "ld_e", "ld_f", "ld_g",
             "st_a", "st_b", "st_c", "st_d", "st_out",
             "mm", "ev_s", "ev_v"]


class Prog:

    def __init__(self, nc, ctx):
        self.h = {n: ctx.enter_context(nc.semaphore(n)) for n in SEM_NAMES}
        self.c = {n: 0 for n in SEM_NAMES}
        self.free = list(range(8))
        self.rr = 0
        self.items = []
        self.bank_prod = {i: ("mm", 0) for i in range(8)}
        self.bank_free = {i: ("mm", 0) for i in range(8)}

    def inc(self, name, k=1):
        self.c[name] += k
        return self.h[name]

    def val(self, name):
        return self.c[name]

    def alloc(self, n):
        assert len(self.free) >= n, f"psum pool exhausted (want {n})"
        order = sorted(self.free, key=lambda b: (b - self.rr) % 8)
        picked = order[:n]
        for b in picked:
            self.free.remove(b)
        self.rr = (picked[-1] + 1) % 8
        need = {}
        for b in picked:
            ctr, v = self.bank_free[b]
            if v > 0:
                need[ctr] = max(need.get(ctr, 0), v)
        for ctr, v in need.items():
            self.wait("tensor", ctr, v)
        return picked

    def release(self, bs):
        for b in bs:
            self.free.append(b)

    def emit(self, eng, fn, inc=None):
        if inc is not None:
            nm, k = inc
            hdl = self.inc(nm, k)

            def wrapped(E, fn=fn, hdl=hdl, k=k):
                fn(E).then_inc(hdl, k)
            self.items.append((eng, wrapped))
        else:
            self.items.append((eng, fn))

    def wait(self, eng, name, valv=None):
        v = self.c[name] if valv is None else valv
        if v <= 0:
            return
        self.emit(eng, lambda E, s=self.h[name], v=v: E.wait_ge(s, v))

    def fence(self):
        for eng in ENGS:
            for ctr in ("mm", "ev_s", "ev_v"):
                self.wait(eng, ctr)

    def flush(self, nc, fence=True):
        if fence:
            self.fence()
        items = self.items
        self.items = []
        with nc.Block() as blk:
            for eng in ENGS:
                fns = [f for (en, f) in items if en == eng]

                def body(E, fns=fns):
                    for f in fns:
                        f(E)
                getattr(blk, eng)(body)


def rec_dma(P, eng, out_ap, in_ap, sem, transpose=False, n_inc=16):
    hdl = P.inc(sem, n_inc)
    if transpose:
        P.emit(eng, lambda E, o=out_ap, i=in_ap, s=hdl, n=n_inc:
               E.dma_start(out=o, in_=i, transpose=True).then_inc(s, n))
    else:
        P.emit(eng, lambda E, o=out_ap, i=in_ap, s=hdl, n=n_inc:
               E.dma_start(out=o, in_=i).then_inc(s, n))
    return P.val(sem)


def rec_quad(P, pb, bset, lhs_of, rhs_of, nk, k0=0):
    for k in range(nk):
        lhs = lhs_of(k0 + k)
        st = (k == 0)
        sp = (k == nk - 1)
        for i, b in enumerate(bset):
            rhs = rhs_of(k0 + k, i)
            if sp:
                P.emit("tensor", lambda E, o=pb[b][:, 0:128], l=lhs, r=rhs, st=st:
                       E.matmul(o, l, r, start=st, stop=True,
                                skip_group_check=True), inc=("mm", 1))
                P.bank_prod[b] = ("mm", P.val("mm"))
            else:
                P.emit("tensor", lambda E, o=pb[b][:, 0:128], l=lhs, r=rhs, st=st:
                       E.matmul(o, l, r, start=st, stop=False,
                                skip_group_check=True))
    return P.val("mm")


def _own_token_index(cid):
    idx = []
    for j in range(8):
        g = 2 * j + cid
        idx.extend(range(128 * g, 128 * g + 128))
    return np.array(idx)


class _StopPhases(Exception):
    pass


def _ck(upto, n):
    if upto < n:
        raise _StopPhases()


def build_nc(taps=(), upto=9):
    nc = bass.Bass("TRN2", target_bir_lowering=False)

    xT_full = nc.dram_tensor("xT_full", [H, S], F32, kind="ExternalInput")
    xT_own = nc.dram_tensor("xT_own", [H, T], F32, kind="ExternalInput")
    wqkvT = nc.dram_tensor("wqkvT", [12, H, 512], F32, kind="ExternalInput")
    woT = nc.dram_tensor("woT", [4, H, 512], F32, kind="ExternalInput")
    gup = nc.dram_tensor("gup", [GU_O, GU_J16], F16, kind="ExternalInput")
    gus = nc.dram_tensor("gus", [16, GU_O], F16, kind="ExternalInput")
    dnp = nc.dram_tensor("dnp", [DN_O, DN_J16], F16, kind="ExternalInput")
    dns = nc.dram_tensor("dns", [DN_KC, DN_O], F16, kind="ExternalInput")
    mask_ev = nc.dram_tensor("mask_ev", [128, 128], BF16, kind="ExternalInput")
    mask_od = nc.dram_tensor("mask_od", [128, 128], BF16, kind="ExternalInput")
    permsel = nc.dram_tensor("permsel", [128, 16, 128], BF16, kind="ExternalInput")
    out_d = nc.dram_tensor("out", [NCH, 128, T], F32, kind="ExternalOutput")

    sgu_scr = nc.dram_tensor("sgu_scr", [16, GU_O], BF16, kind="Internal")
    sdn_scr = nc.dram_tensor("sdn_scr", [DN_KC, DN_O], BF16, kind="Internal")
    wg_scr = nc.dram_tensor("wg_scr", [22, GU_KC, 128, 512], BF16, kind="Internal")
    wd_scr = nc.dram_tensor("wd_scr", [4, DN_KC, 128, 512], BF16, kind="Internal")
    k_scr = nc.dram_tensor("k_scr", [H, S], BF16, kind="Internal")
    v_scr = nc.dram_tensor("v_scr", [S, H], BF16, kind="Internal")
    h_scr = nc.dram_tensor("h_scr", [DN_KC, 128, T], BF16, kind="Internal")
    x1_scr = nc.dram_tensor("x1_scr", [NCH, 128, T], F32, kind="Internal")
    rows_scr = nc.dram_tensor("rows_scr", [24, 1024], F32, kind="Internal")
    negb_scr = nc.dram_tensor("negb_scr", [NH, T], BF16, kind="Internal")
    den_scr = nc.dram_tensor("den_scr", [NH, T], F32, kind="Internal")

    th = {}
    if "wg" in taps: th["tap_wg"] = nc.dram_tensor("tap_wg", [22, GU_KC, 128, 512], F32, kind="ExternalOutput")
    if "wd" in taps: th["tap_wd"] = nc.dram_tensor("tap_wd", [4, DN_KC, 128, 512], F32, kind="ExternalOutput")
    if "kv" in taps:
        th["tap_k"] = nc.dram_tensor("tap_k", [H, S], F32, kind="ExternalOutput")
        th["tap_v"] = nc.dram_tensor("tap_v", [S, H], F32, kind="ExternalOutput")
        th["tap_q"] = nc.dram_tensor("tap_q", [NH, 128, T], F32, kind="ExternalOutput")
    if "attn" in taps: th["tap_attn"] = nc.dram_tensor("tap_attn", [NH, 128, T], F32, kind="ExternalOutput")
    if "nrm" in taps: th["tap_nb"] = nc.dram_tensor("tap_nb", [NH, T], F32, kind="ExternalOutput")
    if "x1" in taps: th["tap_x1"] = nc.dram_tensor("tap_x1", [NCH, 128, T], F32, kind="ExternalOutput")
    if "hact" in taps: th["tap_h"] = nc.dram_tensor("tap_h", [DN_KC, 128, T], F32, kind="ExternalOutput")
    if "y5" in taps: th["tap_y5"] = nc.dram_tensor("tap_y5", [NCH, 128, T], F32, kind="ExternalOutput")

    with ExitStack() as ctx:
        e = ctx.enter_context
        P = Prog(nc, ctx)
        pb = [e(nc.psum_tensor(f"pb{i}", [128, 512], F32)) for i in range(8)]

        bufA = e(nc.sbuf_tensor("bufA", [128, 16, 1024], F32))
        bufB = e(nc.sbuf_tensor("bufB", [128, 16, 1024], BF16))
        bufC = e(nc.sbuf_tensor("bufC", [128, 16, 1024], BF16))
        Q = bufB
        attnb = bufC
        maskt = e(nc.sbuf_tensor("maskt", [128, 256], BF16))
        ones8 = e(nc.sbuf_tensor("ones8", [8, 128], BF16))
        onesd = e(nc.sbuf_tensor("onesd", [128, 8], BF16))
        permt = e(nc.sbuf_tensor("permt", [128, 16, 128], BF16))
        zbias = e(nc.sbuf_tensor("zbias", [128, 1], F32))

        for n in SEM_NAMES:
            P.emit("sync", lambda E, s=P.h[n]: E.sem_clear(s))
        P.flush(nc)
        P.emit("vector", lambda E: E.memset(ones8[:, :], 0.125))
        P.emit("vector", lambda E: E.memset(onesd[:, :], 1.0))
        P.emit("vector", lambda E: E.memset(zbias[:, :], 0.0))
        rec_dma(P, "gpsimd", maskt[:, 0:128], mask_ev[:, :], "ld_a")
        rec_dma(P, "gpsimd", maskt[:, 128:256], mask_od[:, :], "ld_a")
        rec_dma(P, "gpsimd", permt[:, :, :], permsel[:, :, :], "ld_a")
        P.wait("gpsimd", "ld_a")
        P.flush(nc)

        try:
            _ck(upto, 1)
            with nc.named_scope("ph0"):
                _ph0(nc, P, gus, dns, sgu_scr, sdn_scr)
            _ck(upto, 3)
            with nc.named_scope("ph1"):
                _ph1(nc, P, pb, xT_full, xT_own, wqkvT, Q, bufC, bufA,
                     k_scr, v_scr)
            _ck(upto, 5)
            with nc.named_scope("ph2"):
                dstack = ExitStack()
                phd_gen = _phD_gen(nc, P, gup, dnp, sgu_scr, sdn_scr,
                                   wg_scr, wd_scr, dstack)
                _ph2(nc, P, pb, Q, k_scr, v_scr, den_scr, maskt,
                     onesd, ones8, bufA, zbias, phd_gen)
                dstack.close()
            _ck(upto, 6)
            with nc.named_scope("ph2n"):
                _ph2n(nc, P, den_scr, bufA, attnb, rows_scr)
            _ck(upto, 7)
            with nc.named_scope("ph3"):
                _ph3(nc, P, pb, woT, xT_own, attnb, bufA, onesd, permt,
                     rows_scr, x1_scr, bufB, zbias)
            _ck(upto, 8)
            with nc.named_scope("ph4"):
                _ph4(nc, P, pb, wg_scr, bufB, h_scr, zbias)
            _ck(upto, 9)
            with nc.named_scope("ph5"):
                _ph5(nc, P, pb, wd_scr, h_scr, x1_scr, rows_scr, onesd,
                     out_d, zbias, bufA, bufB, bufC)
        except _StopPhases:
            pass

        if th:
            def cp(dst, src):
                rec_dma(P, "gpsimd", dst.ap(), src.ap(), "st_a")
            if "tap_wg" in th: cp(th["tap_wg"], wg_scr)
            if "tap_wd" in th: cp(th["tap_wd"], wd_scr)
            if "tap_k" in th: cp(th["tap_k"], k_scr)
            if "tap_v" in th: cp(th["tap_v"], v_scr)
            if "tap_h" in th: cp(th["tap_h"], h_scr)
            if "tap_x1" in th: cp(th["tap_x1"], x1_scr)
            if "tap_nb" in th: cp(th["tap_nb"], negb_scr)
            if "tap_q" in th:
                for h2 in range(16):
                    rec_dma(P, "gpsimd", th["tap_q"][h2], Q[:, h2, :], "st_a")
            if "tap_attn" in th:
                for h2 in range(16):
                    rec_dma(P, "gpsimd", th["tap_attn"][h2], attnb[:, h2, :], "st_a")
            if "tap_y5" in th:
                for h2 in range(16):
                    rec_dma(P, "gpsimd", th["tap_y5"][h2], bufA[:, h2, :], "st_a")
            P.wait("gpsimd", "st_a")
            P.flush(nc)

    return nc, sorted(th.keys())


def _ph0(nc, P, gus, dns, sgu_scr, sdn_scr):
    with ExitStack() as c2:
        ee = c2.enter_context
        g16 = ee(nc.sbuf_tensor("sc_g16", [16, GU_O], F16))
        d16 = ee(nc.sbuf_tensor("sc_d16", [DN_KC, DN_O], F16))
        gbf = ee(nc.sbuf_tensor("sc_gbf", [16, GU_O], BF16))
        dbf = ee(nc.sbuf_tensor("sc_dbf", [DN_KC, DN_O], BF16))
        rec_dma(P, "gpsimd", g16[:, :], gus[:, :], "ld_a")
        rec_dma(P, "gpsimd", d16[:, :], dns[:, :], "ld_a")
        P.wait("vector", "ld_a")
        P.emit("vector", lambda E: E.tensor_scalar_mul(out=gbf[:, :], in0=g16[:, :],
                                                       scalar1=-0.125))
        P.emit("vector", lambda E: E.tensor_scalar_mul(out=dbf[:, :], in0=d16[:, :],
                                                       scalar1=-0.125),
               inc=("ev_v", 1))
        P.wait("sync", "ev_v")
        rec_dma(P, "sync", sgu_scr[:, :], gbf[:, :], "st_a")
        rec_dma(P, "sync", sdn_scr[:, :], dbf[:, :], "st_a")
        P.wait("sync", "st_a")
        P.flush(nc)


def _phD_gen(nc, P, gup, dnp, sgu_scr, sdn_scr, wg_scr, wd_scr, c2):
    jobs = []
    for (row0, w) in G_SBS:
        for jc in range(4):
            jobs.append(("g", row0, w, jc))
    for jc in range(11):
        jobs.append(("d", 0, 2048, jc))
    import os as _os
    if _os.environ.get("KJOBS"):
        _a, _b = _os.environ["KJOBS"].split(":")
        jobs = jobs[int(_a):int(_b)]

    ee = c2.enter_context
    pk = [ee(nc.sbuf_tensor(f"dq_pk{i}", [128, W_G], U16)) for i in range(2)]
    stl = [ee(nc.sbuf_tensor(f"dq_st{i}", [128, W_G], BF16)) for i in range(2)]
    T = [ee(nc.sbuf_tensor(f"dq_t{i}", [128, W_G], I16)) for i in range(5)]
    wo = [ee(nc.sbuf_tensor(f"dq_wo{i}", [128, W_G], BF16)) for i in range(2)]
    return _phD_body(nc, P, jobs, gup, dnp, sgu_scr, sdn_scr, wg_scr, wd_scr,
                     pk, stl, T, wo)


def _phD_body(nc, P, jobs, gup, dnp, sgu_scr, sdn_scr, wg_scr, wd_scr,
              pk, stl, T, wo):
    base_st = P.val("st_b")
    chain_done = {}
    for ji, (kind, row0, w, jc) in enumerate(jobs):
        par = ji % 2
        if ji >= 2:
            gate = chain_done[4 * ji - 5]
            P.wait("sync", "ev_v", gate)
            P.wait("gpsimd", "ev_v", gate)
        src = gup if kind == "g" else dnp
        ldb = rec_dma(P, "sync", pk[par].bitcast(F16)[:, 0:w],
                      src[row0:row0 + w, 128 * jc:128 * (jc + 1)],
                      "ld_b", transpose=True)
        scr = sgu_scr if kind == "g" else sdn_scr
        ldc = None
        for band in range(4):
            ldc = rec_dma(P, "gpsimd", stl[par][32 * band:32 * (band + 1), 0:w],
                          scr[4 * jc + band:4 * jc + band + 1, row0:row0 + w]
                          .partition_broadcast(32), "ld_c")
        P.wait("vector", "ld_b", ldb)
        P.wait("vector", "ld_c", ldc)
        for r in range(4):
            cidx = 4 * ji + r
            wpar = cidx % 2
            if cidx >= 2:
                P.wait("vector", "st_b", base_st + 64 * (cidx - 1))
            ww = w

            def V(f, inc=None):
                P.emit("vector", f, inc=inc)
            if r == 0:
                V(lambda E, p=pk[par], w=ww: E.tensor_single_scalar(
                    out=T[0].bitcast(U16)[:, 0:w], in_=p[:, 0:w],
                    scalar=15, op=OP.bitwise_and))
            else:
                V(lambda E, p=pk[par], w=ww, r=r: E.tensor_scalar(
                    out=T[0].bitcast(U16)[:, 0:w], in0=p[:, 0:w],
                    scalar1=4 * r, scalar2=15,
                    op0=OP.logical_shift_right, op1=OP.bitwise_and))
            V(lambda E, w=ww: E.tensor_single_scalar(
                out=T[1][:, 0:w], in_=T[0][:, 0:w], scalar=7,
                op=OP.bitwise_and))
            V(lambda E, w=ww: E.tensor_single_scalar(
                out=T[2][:, 0:w], in_=T[0][:, 0:w], scalar=8,
                op=OP.bitwise_and))
            yield
            V(lambda E, w=ww: E.tensor_scalar(
                out=T[0][:, 0:w], in0=T[1][:, 0:w], scalar1=4,
                scalar2=0, op0=OP.subtract, op1=OP.max))
            V(lambda E, w=ww: E.tensor_scalar(
                out=T[3][:, 0:w], in0=T[1][:, 0:w], scalar1=6,
                scalar2=0, op0=OP.subtract, op1=OP.max))
            V(lambda E, w=ww: E.tensor_tensor(
                out=T[4][:, 0:w], in0=T[1][:, 0:w],
                in1=T[0][:, 0:w], op=OP.add))
            yield
            V(lambda E, w=ww: E.scalar_tensor_tensor(
                out=T[1][:, 0:w], in0=T[3][:, 0:w], scalar=2,
                in1=T[4][:, 0:w], op0=OP.mult, op1=OP.add))
            V(lambda E, w=ww: E.scalar_tensor_tensor(
                out=T[3][:, 0:w], in0=T[2][:, 0:w], scalar=4,
                in1=T[1][:, 0:w], op0=OP.subtract, op1=OP.mult))
            V(lambda E, w=ww, wp=wpar, p=par: E.tensor_tensor(
                out=wo[wp][:, 0:w], in0=T[3][:, 0:w], in1=stl[p][:, 0:w],
                op=OP.mult), inc=("ev_v", 1))
            chain_done[cidx] = P.val("ev_v")
            P.wait("gpsimd", "ev_v")
            kc = 4 * jc + r
            npieces = w // 512
            for pi in range(npieces):
                og = (row0 + 512 * pi) // 512
                dst = wg_scr[og, kc] if kind == "g" else wd_scr[og, kc]
                n_inc = 16 if pi < npieces - 1 else 16 * (4 - npieces + 1)
                rec_dma(P, "gpsimd", dst[:, :],
                        wo[wpar][:, 512 * pi:512 * (pi + 1)], "st_b",
                        n_inc=n_inc)
            yield
    P.wait("gpsimd", "st_b")


def _ph1(nc, P, pb, xT_full, xT_own, wqkvT, Q, xo, bufA, k_scr, v_scr):
    xf = bufA.bitcast(BF16)
    with ExitStack() as c2:
        ee = c2.enter_context
        ws = [ee(nc.sbuf_tensor(f"wq{i}", [128, NCH, 512], BF16)) for i in range(2)]
        stg = [ee(nc.sbuf_tensor(f"kvstg{i}", [128, 512], BF16)) for i in range(4)]

        tiles = []
        for s in range(12):
            kind = "q" if s < 4 else ("k" if s < 8 else "v")
            if kind == "q":
                tiles += [(s, kind, oc, tp) for oc in range(4) for tp in range(2)]
            elif kind == "k":
                tiles += [(s, kind, oc, tp) for oc in range(4) for tp in range(4)]
            else:
                tiles += [(s, kind, tch, 0) for tch in range(16)]
        slab_last = {}
        for i, tl in enumerate(tiles):
            slab_last[tl[0]] = i

        lda = rec_dma(P, "gpsimd", xo[:, :, :],
                      xT_own.reshape([NCH, 128, T]).ap().transpose([1, 0, 2]), "ld_a")
        lda = rec_dma(P, "gpsimd", xf[:, :, :],
                      xT_full.reshape([NCH, 128, S]).ap().transpose([1, 0, 2]), "ld_a")
        P.wait("tensor", "ld_a", lda)
        slab_ld = {}
        for s in range(2):
            slab_ld[s] = rec_dma(P, "gpsimd", ws[s % 2][:, :, :],
                                 wqkvT.reshape([12, NCH, 128, 512])[s]
                                 .transpose([1, 0, 2]), "ld_b")
        nkv = 0
        base_stc = P.val("st_c")
        for ti, (s, kind, a, tp) in enumerate(tiles):
            par = s % 2
            P.wait("tensor", "ld_b", slab_ld[s])
            bset = P.alloc(4)
            if kind == "q":
                rec_quad(P, pb, bset,
                         lambda k, par=par, a=a: ws[par][:, k, 128 * a:128 * (a + 1)],
                         lambda k, sl, tp=tp: xo[:, k, 512 * tp + 128 * sl:512 * tp + 128 * sl + 128],
                         16)
            elif kind == "k":
                rec_quad(P, pb, bset,
                         lambda k, par=par, a=a: ws[par][:, k, 128 * a:128 * (a + 1)],
                         lambda k, sl, tp=tp: xf[:, k, 512 * tp + 128 * sl:512 * tp + 128 * sl + 128],
                         16)
            else:
                rec_quad(P, pb, bset,
                         lambda k, a=a: xf[:, k, 128 * a:128 * (a + 1)],
                         lambda k, sl, par=par: ws[par][:, k, 128 * sl:128 * (sl + 1)],
                         16)
            P.release(bset)
            mm_here = P.val("mm")
            if ti == slab_last[s] and s + 2 <= 11:
                P.wait("gpsimd", "mm", mm_here)
                slab_ld[s + 2] = rec_dma(P, "gpsimd", ws[(s + 2) % 2][:, :, :],
                                         wqkvT.reshape([12, NCH, 128, 512])[s + 2]
                                         .transpose([1, 0, 2]), "ld_b")
            P.wait("scalar", "mm", mm_here)
            if kind == "q":
                head = 4 * s + a
                for i, b in enumerate(bset):
                    inc = ("ev_s", 1) if i == 3 else None
                    P.emit("scalar", lambda E, b=b, head=head, tp=tp, i=i:
                           E.mul(out=Q[:, head, 512 * tp + 128 * i:512 * tp + 128 * (i + 1)],
                                 in_=pb[b][:, 0:128], mul=SCALE), inc=inc)
            else:
                spar = nkv % 4
                if nkv >= 4:
                    P.wait("scalar", "st_c", base_stc + 16 * (nkv - 3))
                for i, b in enumerate(bset):
                    inc = ("ev_s", 1) if i == 3 else None
                    P.emit("scalar", lambda E, b=b, spar=spar, i=i:
                           E.copy(out=stg[spar][:, 128 * i:128 * (i + 1)],
                                  in_=pb[b][:, 0:128]), inc=inc)
                P.wait("sync", "ev_s")
                if kind == "k":
                    o0 = 512 * (s - 4) + 128 * a
                    rec_dma(P, "sync", k_scr[o0:o0 + 128, 512 * tp:512 * (tp + 1)],
                            stg[spar][:, :], "st_c")
                else:
                    rec_dma(P, "sync", v_scr[128 * a:128 * (a + 1),
                                             512 * (s - 8):512 * (s - 7)],
                            stg[spar][:, :], "st_c")
                nkv += 1
            evv = P.val("ev_s")
            for b in bset:
                P.bank_free[b] = ("ev_s", evv)
        P.wait("sync", "st_c")
        P.flush(nc)


def _ph2a(nc, P, pb, Q, k_scr, negb_scr, onesd, zbias):
    with ExitStack() as c2:
        ee = c2.enter_context
        kb = [ee(nc.sbuf_tensor(f"nk{i}", [128, 2048], BF16)) for i in range(2)]
        sqb = [ee(nc.sbuf_tensor(f"nsq{i}", [128, 2048], BF16)) for i in range(2)]
        kmp = ee(nc.sbuf_tensor("kmp", [1, 64], F32))
        kmax = ee(nc.sbuf_tensor("kmax", [1, 16], F32))
        qrow = [ee(nc.sbuf_tensor(f"qrow{i}", [1, 1024], F32)) for i in range(2)]
        nrow = [ee(nc.sbuf_tensor(f"nrow{i}", [1, 1024], BF16)) for i in range(2)]

        sq_done = {}
        mm_done = {}
        for h2 in range(16):
            if h2 >= 2:
                P.wait("sync", "ev_s", sq_done[h2 - 2])
            ldd = rec_dma(P, "sync", kb[h2 % 2][:, :],
                          k_scr[128 * h2:128 * (h2 + 1), :], "ld_d")
            P.wait("scalar", "ld_d", ldd)
            if h2 >= 2:
                P.wait("scalar", "mm", mm_done[h2 - 2])
            P.emit("scalar", lambda E, p=h2 % 2: E.activation(
                out=sqb[p][:, :], in_=kb[p][:, :], func=AF.Square,
                bias=zbias[:, :]), inc=("ev_s", 1))
            sq_done[h2] = P.val("ev_s")
            P.wait("tensor", "ev_s", sq_done[h2])
            for piece in range(4):
                bb = P.alloc(1)[0]
                P.emit("tensor", lambda E, bb=bb, p=h2 % 2, piece=piece:
                       E.matmul(pb[bb][0:8, 0:512], onesd[:, :],
                                sqb[p][:, 512 * piece:512 * (piece + 1)],
                                start=True, stop=True, skip_group_check=True),
                       inc=("mm", 1))
                mmv = P.val("mm")
                P.release([bb])
                P.wait("vector", "mm", mmv)
                P.emit("vector", lambda E, bb=bb, h2=h2, piece=piece: E.reduce_max(
                    out=kmp[0:1, 4 * h2 + piece:4 * h2 + piece + 1],
                    in_=pb[bb][0:1, 0:512], axis=AXX), inc=("ev_v", 1))
                P.bank_free[bb] = ("ev_v", P.val("ev_v"))
            mm_done[h2] = P.val("mm")
        for h2 in range(16):
            inc = ("ev_v", 1) if h2 == 15 else None
            P.emit("vector", lambda E, h2=h2: E.reduce_max(
                out=kmax[0:1, h2:h2 + 1], in_=kmp[0:1, 4 * h2:4 * h2 + 4],
                axis=AXX), inc=inc)
        P.wait("scalar", "ev_v")
        P.emit("scalar", lambda E: E.activation(out=kmax[0:1, :], in_=kmax[0:1, :],
                                                func=AF.Sqrt, bias=zbias[0:1, :]),
               inc=("ev_s", 1))
        mm2_done = {}
        nst = {}
        for h2 in range(16):
            if h2 >= 2:
                P.wait("scalar", "mm", mm2_done[h2 - 2])
            else:
                P.wait("scalar", "mm", mm_done[15 if h2 == 1 else 14])
            P.emit("scalar", lambda E, p=h2 % 2, h2=h2: E.activation(
                out=sqb[p][:, 0:1024], in_=Q[:, h2, :], func=AF.Square,
                bias=zbias[:, :]), inc=("ev_s", 1))
            sqv = P.val("ev_s")
            P.wait("tensor", "ev_s", sqv)
            for piece in range(2):
                bb = P.alloc(1)[0]
                P.emit("tensor", lambda E, bb=bb, p=h2 % 2, piece=piece:
                       E.matmul(pb[bb][0:8, 0:512], onesd[:, :],
                                sqb[p][:, 512 * piece:512 * (piece + 1)],
                                start=True, stop=True, skip_group_check=True),
                       inc=("mm", 1))
                mmv = P.val("mm")
                P.release([bb])
                P.wait("vector", "mm", mmv)
                P.emit("vector", lambda E, bb=bb, h2=h2, piece=piece: E.tensor_copy(
                    out=qrow[h2 % 2][0:1, 512 * piece:512 * (piece + 1)],
                    in_=pb[bb][0:1, 0:512]), inc=("ev_v", 1))
                P.bank_free[bb] = ("ev_v", P.val("ev_v"))
            mm2_done[h2] = P.val("mm")
            P.wait("scalar", "ev_v")
            P.emit("scalar", lambda E, p=h2 % 2: E.activation(
                out=qrow[p][0:1, :], in_=qrow[p][0:1, :], func=AF.Sqrt,
                bias=zbias[0:1, :]), inc=("ev_s", 1))
            P.wait("vector", "ev_s")
            if h2 >= 2:
                P.wait("vector", "st_a", nst[h2 - 2])
            P.emit("vector", lambda E, p=h2 % 2, h2=h2: E.tensor_scalar(
                out=nrow[p][0:1, :], in0=qrow[p][0:1, :],
                scalar1=kmax[0:1, h2:h2 + 1], scalar2=-1.0,
                op0=OP.mult, op1=OP.mult), inc=("ev_v", 1))
            P.wait("sync", "ev_v")
            nst[h2] = rec_dma(P, "sync", negb_scr[h2:h2 + 1, :],
                              nrow[h2 % 2][0:1, :], "st_a")
        P.wait("sync", "st_a")
        P.flush(nc)


def _ph2(nc, P, pb, Q, k_scr, v_scr, den_scr, maskt, onesd, ones8,
         araw, zbias, phd_gen=None):
    with ExitStack() as c2:
        ee = c2.enter_context
        ksb = [ee(nc.sbuf_tensor(f"ksb{i}", [128, 2048], BF16)) for i in range(2)]
        vsb = [ee(nc.sbuf_tensor(f"vsb{i}", [128, 16, 128], BF16)) for i in range(2)]
        psb = [ee(nc.sbuf_tensor(f"psb{i}", [128, 512], BF16)) for i in range(2)]
        denh = [ee(nc.sbuf_tensor(f"denh{i}", [1, 512], F32)) for i in range(2)]

        def load_kv(h2):
            rec_dma(P, "sync", ksb[h2 % 2][:, :],
                    k_scr[128 * h2:128 * (h2 + 1), :], "ld_e")
            return rec_dma(P, "sync", vsb[h2 % 2][:, :, :],
                           v_scr.reshape([16, 128, H]).ap()
                           [:, :, 128 * h2:128 * (h2 + 1)].transpose([1, 0, 2]),
                           "ld_e")

        kv_ld = {h2: load_kv(h2) for h2 in range(2)}
        psb_guard = []
        hq_mm = {}
        den_st = {}
        pump_acc = [0.0]
        pump_rate = 420.0 / 384.0

        def pump():
            if phd_gen is None:
                return
            pump_acc[0] += pump_rate
            while pump_acc[0] >= 1.0:
                pump_acc[0] -= 1.0
                if next(phd_gen, "END") == "END":
                    return
        for h2 in range(16):
            for qh in range(2):
                it = 2 * h2 + qh
                dpar = it % 2
                if qh == 0:
                    P.wait("tensor", "ld_e", kv_ld[h2])
                cmax = 8 * qh + 8
                pvb = P.alloc(4)
                dnb = None
                dn_ql0 = 0
                sc_bank = {}
                sc_mmv = {}

                def scores(c):
                    ql = max(0, 128 * (c // 2) - 512 * qh)
                    qlen = 512 - ql
                    sb = P.alloc(1)[0]
                    P.emit("tensor", lambda E, sb=sb, h2=h2, c=c, qh=qh, ql=ql, qlen=qlen:
                           E.matmul(pb[sb][:, 0:qlen],
                                    ksb[h2 % 2][:, 128 * c:128 * (c + 1)],
                                    Q[:, h2, 512 * qh + ql:512 * qh + ql + qlen],
                                    start=True, stop=True, skip_group_check=True),
                           inc=("mm", 1))
                    sc_bank[c] = sb
                    sc_mmv[c] = P.val("mm")

                scores(0)
                if cmax > 1:
                    scores(1)
                for c in range(cmax):
                    ql = max(0, 128 * (c // 2) - 512 * qh)
                    qlen = 512 - ql
                    if c + 2 < cmax:
                        scores(c + 2)
                    sb = sc_bank[c]
                    g = sc_mmv[c]
                    if len(psb_guard) >= 2:
                        g = max(g, psb_guard[-2])
                    P.wait("scalar", "mm", g)
                    diag = (c >= 8 * qh)
                    P.emit("scalar", lambda E, sb=sb, c=c, qlen=qlen:
                           E.activation(out=psb[c % 2][:, 0:qlen],
                                        in_=pb[sb][:, 0:qlen], func=AF.Exp,
                                        bias=zbias[:, :]), inc=("ev_s", 1))
                    P.bank_free[sb] = ("ev_s", P.val("ev_s"))
                    P.release([sb])
                    if diag:
                        P.wait("vector", "ev_s")
                        mo = 0 if c % 2 == 0 else 128
                        P.emit("vector", lambda E, c=c, mo=mo: E.tensor_tensor(
                            out=psb[c % 2][:, 0:128], in0=psb[c % 2][:, 0:128],
                            in1=maskt[:, mo:mo + 128], op=OP.mult),
                            inc=("ev_v", 1))
                        P.wait("tensor", "ev_v")
                    else:
                        P.wait("tensor", "ev_s")
                    if c % 4 == 0:
                        dnb = P.alloc(1)[0]
                        dn_ql0 = ql
                    first_sl = ql // 128
                    for sl in range(4):
                        if sl < first_sl:
                            continue
                        stop_c = 2 * (sl + 4 * qh) + 1
                        if c == stop_c:
                            P.emit("tensor", lambda E, b=pvb[sl], h2=h2, c=c, sl=sl, ql=ql:
                                   E.matmul(pb[b][:, 0:128], vsb[h2 % 2][:, c, :],
                                            psb[c % 2][:, 128 * sl - ql:128 * sl - ql + 128],
                                            start=(c == 0), stop=True,
                                            skip_group_check=True), inc=("mm", 1))
                            P.bank_prod[pvb[sl]] = ("mm", P.val("mm"))
                        else:
                            P.emit("tensor", lambda E, b=pvb[sl], h2=h2, c=c, sl=sl, ql=ql:
                                   E.matmul(pb[b][:, 0:128], vsb[h2 % 2][:, c, :],
                                            psb[c % 2][:, 128 * sl - ql:128 * sl - ql + 128],
                                            start=(c == 0), stop=False,
                                            skip_group_check=True))
                    ep_end = (c % 4 == 3) or (c == cmax - 1)
                    P.emit("tensor", lambda E, dnb=dnb, ql=ql, qlen=qlen, c=c, ep=ep_end:
                           E.matmul(pb[dnb][0:8, ql:512], onesd[:, :],
                                    psb[c % 2][:, 0:qlen], start=(c % 4 == 0),
                                    stop=ep, skip_group_check=True), inc=("mm", 1))
                    psb_guard.append(P.val("mm"))
                    if ep_end:
                        P.wait("vector", "mm", P.val("mm"))
                        if c < 4:
                            if it >= 2:
                                P.wait("vector", "st_a", den_st[it - 2])
                            P.emit("vector", lambda E, db=dnb, dp=dpar:
                                   E.tensor_copy(out=denh[dp][0:1, :],
                                                 in_=pb[db][0:1, 0:512]),
                                   inc=("ev_v", 1))
                        else:
                            P.emit("vector", lambda E, db=dnb, dp=dpar, ql0=dn_ql0:
                                   E.tensor_tensor(
                                       out=denh[dp][0:1, ql0:512],
                                       in0=denh[dp][0:1, ql0:512],
                                       in1=pb[db][0:1, ql0:512], op=OP.add),
                                   inc=("ev_v", 1))
                        P.bank_free[dnb] = ("ev_v", P.val("ev_v"))
                        P.release([dnb])
                    pump()
                P.wait("sync", "ev_v")
                den_st[it] = rec_dma(P, "sync",
                                     den_scr[h2:h2 + 1, 512 * qh:512 * (qh + 1)],
                                     denh[dpar][0:1, :], "st_a")
                hq_mm[(h2, qh)] = P.val("mm")
                if qh == 1 and h2 + 2 <= 15:
                    P.wait("sync", "mm", hq_mm[(h2, 1)])
                    kv_ld[h2 + 2] = load_kv(h2 + 2)
                P.wait("scalar", "mm", max(P.bank_prod[b][1] for b in pvb))
                for sl in range(4):
                    b = pvb[sl]
                    inc = ("ev_s", 1) if sl == 3 else None
                    P.emit("scalar", lambda E, b=b, h2=h2, qh=qh, sl=sl: E.copy(
                        out=araw[:, h2, 512 * qh + 128 * sl:512 * qh + 128 * (sl + 1)],
                        in_=pb[b][:, 0:128]), inc=inc)
                evv = P.val("ev_s")
                for b in pvb:
                    P.bank_free[b] = ("ev_s", evv)
                P.release(pvb)
        if phd_gen is not None:
            for _ in phd_gen:
                pass
        P.flush(nc)


def _ph2n(nc, P, den_scr, araw, attnb, rows_scr):
    with ExitStack() as c2:
        ee = c2.enter_context
        den16 = ee(nc.sbuf_tensor("den16", [16, 1024], F32))
        denr = ee(nc.sbuf_tensor("denr", [16, 1024], F32))
        rcb = [ee(nc.sbuf_tensor(f"rcb{i}", [128, 1024], F32)) for i in range(2)]
        ldd = rec_dma(P, "sync", den16[:, :], den_scr[:, :], "ld_f")
        P.wait("vector", "ld_f", ldd)
        P.emit("vector", lambda E: E.reciprocal(out=denr[:, :], in_=den16[:, :]),
               inc=("ev_v", 1))
        P.wait("sync", "ev_v")
        sta = rec_dma(P, "sync", rows_scr[0:16, :], denr[:, :], "st_a")
        P.wait("sync", "st_a", sta)
        norm_done = {}
        for h2 in range(16):
            if h2 >= 2:
                P.wait("sync", "ev_v", norm_done[h2 - 2])
            ldf = rec_dma(P, "sync", rcb[h2 % 2][:, :],
                          rows_scr[h2:h2 + 1, :].partition_broadcast(128), "ld_f")
            P.wait("vector", "ld_f", ldf)
            P.emit("vector", lambda E, h2=h2: E.tensor_tensor(
                out=attnb[:, h2, :], in0=araw[:, h2, :], in1=rcb[h2 % 2][:, :],
                op=OP.mult), inc=("ev_v", 1))
            norm_done[h2] = P.val("ev_v")
        P.flush(nc)


SUMC, SQC, T1C, T2C = 0, 1024, 2048, 3072


def _stats_rows(nc, P, pb, src, bfb, sqbufs, onesd, srows, zbias, want_sum):
    P.wait("scalar", "ev_v")
    hv = {}
    for ep in range(4):
        nb = 4 if want_sum else 2
        banks = P.alloc(nb)
        if want_sum:
            bsum = banks[0:2]
            bsq = banks[2:4]
        else:
            bsq = banks[0:2]
        for hch in range(4 * ep, 4 * ep + 4):
            pbuf = bfb[hch % 2] if want_sum else None
            sqv = sqbufs[hch % 2]
            if hch >= 2:
                P.wait("vector", "mm", hv[hch - 2])
                P.wait("scalar", "mm", hv[hch - 2])
            if want_sum:
                P.emit("vector", lambda E, hch=hch, pbuf=pbuf: E.tensor_copy(
                    out=pbuf[:, :], in_=src[:, hch, :]), inc=("ev_v", 1))
                P.wait("tensor", "ev_v")
            P.emit("scalar", lambda E, hch=hch, sq=sqv: E.activation(
                out=sq[:, :], in_=src[:, hch, :], func=AF.Square,
                bias=zbias[:, :]), inc=("ev_s", 1))
            P.wait("tensor", "ev_s")
            st = (hch == 4 * ep)
            sp = (hch == 4 * ep + 3)
            pairs = ([(bsum, pbuf)] if want_sum else []) + [(bsq, sqv)]
            for pi, (bks, dat) in enumerate(pairs):
                last_pair = (pi == len(pairs) - 1)
                for tp in range(2):
                    inc = ("mm", 1) if (sp or (last_pair and tp == 1)) else None
                    P.emit("tensor", lambda E, b=bks[tp], dat=dat, tp=tp, st=st, sp=sp:
                           E.matmul(pb[b][0:8, 0:512], onesd[:, :],
                                    dat[:, 512 * tp:512 * (tp + 1)],
                                    start=st, stop=sp, skip_group_check=True),
                           inc=inc)
                    if sp:
                        P.bank_prod[bks[tp]] = ("mm", P.val("mm"))
            hv[hch] = P.val("mm")
        todo = ([(bsum, SUMC)] if want_sum else []) + [(bsq, SQC)]
        P.wait("vector", "mm", max(P.bank_prod[b][1] for b in banks))
        for (bks, col) in todo:
            for tp in range(2):
                b = bks[tp]
                if ep == 0:
                    P.emit("vector", lambda E, b=b, col=col, tp=tp: E.tensor_copy(
                        out=srows[0:1, col + 512 * tp:col + 512 * (tp + 1)],
                        in_=pb[b][0:1, 0:512]), inc=("ev_v", 1))
                else:
                    P.emit("vector", lambda E, b=b, col=col, tp=tp: E.tensor_tensor(
                        out=srows[0:1, col + 512 * tp:col + 512 * (tp + 1)],
                        in0=srows[0:1, col + 512 * tp:col + 512 * (tp + 1)],
                        in1=pb[b][0:1, 0:512], op=OP.add), inc=("ev_v", 1))
                P.bank_free[b] = ("ev_v", P.val("ev_v"))
        P.release(banks)


def _newton_rsqrt(nc, P, srows, zbias, vc, yc, tc):
    P.emit("vector", lambda E: E.tensor_copy(out=srows[0:1, tc:tc + 1024],
                                             in_=srows[0:1, vc:vc + 1024]),
           inc=("ev_v", 1))
    P.wait("scalar", "ev_v")
    P.emit("scalar", lambda E: E.activation(out=srows[0:1, tc:tc + 1024],
                                            in_=srows[0:1, tc:tc + 1024],
                                            func=AF.Sqrt, bias=zbias[0:1, :]),
           inc=("ev_s", 1))
    P.wait("vector", "ev_s")
    P.emit("vector", lambda E: E.reciprocal(out=srows[0:1, yc:yc + 1024],
                                            in_=srows[0:1, tc:tc + 1024]))
    P.emit("vector", lambda E: E.tensor_tensor(out=srows[0:1, tc:tc + 1024],
                                               in0=srows[0:1, yc:yc + 1024],
                                               in1=srows[0:1, yc:yc + 1024],
                                               op=OP.mult))
    P.emit("vector", lambda E: E.tensor_tensor(out=srows[0:1, tc:tc + 1024],
                                               in0=srows[0:1, tc:tc + 1024],
                                               in1=srows[0:1, vc:vc + 1024],
                                               op=OP.mult))
    P.emit("vector", lambda E: E.tensor_scalar(out=srows[0:1, tc:tc + 1024],
                                               in0=srows[0:1, tc:tc + 1024],
                                               scalar1=-0.5, scalar2=1.5,
                                               op0=OP.mult, op1=OP.add))
    P.emit("vector", lambda E: E.tensor_tensor(out=srows[0:1, yc:yc + 1024],
                                               in0=srows[0:1, yc:yc + 1024],
                                               in1=srows[0:1, tc:tc + 1024],
                                               op=OP.mult))


def _rows_finish(nc, P, srows, rows_scr, zbias, mrow, rrow, eps, nf):
    P.emit("vector", lambda E: E.tensor_scalar_mul(
        out=srows[0:1, SUMC:SUMC + 1024], in0=srows[0:1, SUMC:SUMC + 1024],
        scalar1=1.0 / nf))
    P.emit("vector", lambda E: E.tensor_scalar_mul(
        out=srows[0:1, SQC:SQC + 1024], in0=srows[0:1, SQC:SQC + 1024],
        scalar1=1.0 / nf))
    P.emit("vector", lambda E: E.tensor_tensor(
        out=srows[0:1, T1C:T1C + 1024], in0=srows[0:1, SUMC:SUMC + 1024],
        in1=srows[0:1, SUMC:SUMC + 1024], op=OP.mult))
    P.emit("vector", lambda E: E.tensor_tensor(
        out=srows[0:1, SQC:SQC + 1024], in0=srows[0:1, SQC:SQC + 1024],
        in1=srows[0:1, T1C:T1C + 1024], op=OP.subtract))
    P.emit("vector", lambda E, eps=eps: E.tensor_scalar_add(
        out=srows[0:1, SQC:SQC + 1024], in0=srows[0:1, SQC:SQC + 1024],
        scalar1=eps))
    _newton_rsqrt(nc, P, srows, zbias, SQC, T1C, T2C)
    P.emit("vector", lambda E: E.tensor_copy(out=srows[0:1, T2C:T2C + 1024],
                                             in_=srows[0:1, T1C:T1C + 1024]),
           inc=("ev_v", 1))
    P.wait("sync", "ev_v")
    rec_dma(P, "sync", rows_scr[mrow:mrow + 1, :], srows[0:1, SUMC:SUMC + 1024],
            "st_a")
    rec_dma(P, "sync", rows_scr[rrow:rrow + 1, :], srows[0:1, T1C:T1C + 1024],
            "st_a")
    P.wait("sync", "st_a")


def _rows_rms(nc, P, srows, rows_scr, zbias, dstrow, eps, nf):
    P.emit("vector", lambda E, eps=eps: E.tensor_scalar(
        out=srows[0:1, SQC:SQC + 1024], in0=srows[0:1, SQC:SQC + 1024],
        scalar1=1.0 / nf, scalar2=eps, op0=OP.mult, op1=OP.add))
    _newton_rsqrt(nc, P, srows, zbias, SQC, T1C, T2C)
    P.emit("vector", lambda E: E.tensor_copy(out=srows[0:1, T2C:T2C + 1024],
                                             in_=srows[0:1, T1C:T1C + 1024]),
           inc=("ev_v", 1))
    P.wait("sync", "ev_v")
    rec_dma(P, "sync", rows_scr[dstrow:dstrow + 1, :],
            srows[0:1, T1C:T1C + 1024], "st_a")
    P.wait("sync", "st_a")


def _ph3(nc, P, pb, woT, xT_own, attnb, xp, onesd, permt, rows_scr,
         x1_scr, x1p, zbias):
    with ExitStack() as c2:
        ee = c2.enter_context
        wos = [ee(nc.sbuf_tensor(f"wo{i}", [128, NCH, 512], BF16)) for i in range(2)]
        xfo = [ee(nc.sbuf_tensor(f"xfo{i}", [128, 1024], F32)) for i in range(2)]
        xbfb = [ee(nc.sbuf_tensor(f"xbf{i}", [128, 1024], BF16)) for i in range(2)]
        xsqb = [ee(nc.sbuf_tensor(f"xsq{i}", [128, 1024], BF16)) for i in range(2)]
        x1bf = ee(nc.sbuf_tensor("x1bf", [128, 4, 1024], BF16))
        mbc = ee(nc.sbuf_tensor("mbc", [128, 1024], F32))
        rbc = ee(nc.sbuf_tensor("rbc", [128, 1024], F32))
        ostg = [ee(nc.sbuf_tensor(f"ostg{i}", [128, 512], F32)) for i in range(2)]
        srows = wos[0].bitcast(F32).reshape([128, 4096])

        def load_slab(s4):
            return rec_dma(P, "gpsimd", wos[s4 % 2][:, :, :],
                           woT.reshape([4, NCH, 128, 512])[s4]
                           .transpose([1, 0, 2]), "ld_b")

        def load_xfo(och):
            return rec_dma(P, "gpsimd", xfo[och % 2][:, :],
                           xT_own[128 * och:128 * (och + 1), :], "ld_g")

        import os as _os
        nopf = _os.environ.get("NOPF", "1") == "1"
        noxf = bool(_os.environ.get("NOXF"))
        slab_ld = {0: load_slab(0)} if nopf else {s4: load_slab(s4)
                                                  for s4 in range(2)}
        och_ld = {} if noxf else {och: load_xfo(och) for och in range(2)}
        och_done = {}
        add_done = {}
        for och in range(16):
            s4, oc = och // 4, och % 4
            if nopf and oc == 0 and s4 >= 1:
                P.wait("gpsimd", "ev_v", och_done[och - 1])
                slab_ld[s4] = load_slab(s4)
            if noxf:
                if och >= 2:
                    P.wait("gpsimd", "ev_v", och_done[och - 2])
                och_ld[och] = load_xfo(och)
            for tp in range(2):
                it = 2 * och + tp
                P.wait("tensor", "ld_b", slab_ld[s4])
                bset = P.alloc(4)
                rec_quad(P, pb, bset,
                         lambda k, s4=s4, oc=oc: wos[s4 % 2][:, k, 128 * oc:128 * (oc + 1)],
                         lambda k, sl, tp=tp: attnb[:, k, 512 * tp + 128 * sl:512 * tp + 128 * sl + 128],
                         16)
                P.release(bset)
                P.wait("scalar", "mm", max(P.bank_prod[b][1] for b in bset))
                if it >= 2:
                    P.wait("scalar", "ev_v", add_done[it - 2])
                for i, b in enumerate(bset):
                    inc = ("ev_s", 1) if i == 3 else None
                    P.emit("scalar", lambda E, b=b, it=it, i=i: E.copy(
                        out=ostg[it % 2][:, 128 * i:128 * (i + 1)],
                        in_=pb[b][:, 0:128]), inc=inc)
                stv = P.val("ev_s")
                for b in bset:
                    P.bank_free[b] = ("ev_s", stv)
                P.wait("vector", "ev_s", stv)
                P.wait("vector", "ld_g", och_ld[och])
                P.emit("vector", lambda E, och=och, tp=tp, it=it: E.tensor_tensor(
                    out=xp[:, och, 512 * tp:512 * (tp + 1)],
                    in0=ostg[it % 2][:, 0:512],
                    in1=xfo[och % 2][:, 512 * tp:512 * (tp + 1)],
                    op=OP.add), inc=("ev_v", 1))
                add_done[it] = P.val("ev_v")
            och_done[och] = P.val("ev_v")
            if not noxf and och + 2 <= 15:
                P.wait("gpsimd", "ev_v", och_done[och])
                och_ld[och + 2] = load_xfo(och + 2)
            if not nopf and oc == 3 and s4 + 2 <= 3:
                if noxf or och + 2 > 15:
                    P.wait("gpsimd", "ev_v", och_done[och])
                slab_ld[s4 + 2] = load_slab(s4 + 2)
        _stats_rows(nc, P, pb, xp, xbfb, xsqb, onesd, srows, zbias, True)
        _rows_finish(nc, P, srows, rows_scr, zbias, 16, 17, LN_EPS, 2048.0)
        rec_dma(P, "sync", mbc[:, :], rows_scr[16:17, :].partition_broadcast(128),
                "ld_f")
        ldr = rec_dma(P, "sync", rbc[:, :],
                      rows_scr[17:18, :].partition_broadcast(128), "ld_f")
        P.wait("vector", "ld_f", ldr)
        for jc in range(4):
            casts = {}
            for u in range(4):
                hch = 4 * jc + u
                P.emit("vector", lambda E, hch=hch: E.tensor_tensor(
                    out=xp[:, hch, :], in0=xp[:, hch, :], in1=mbc[:, :],
                    op=OP.subtract))
                P.emit("vector", lambda E, hch=hch: E.tensor_tensor(
                    out=xp[:, hch, :], in0=xp[:, hch, :], in1=rbc[:, :],
                    op=OP.mult))
                P.emit("vector", lambda E, hch=hch, u=u: E.tensor_copy(
                    out=x1bf[:, u, :], in_=xp[:, hch, :]), inc=("ev_v", 1))
                casts[u] = P.val("ev_v")
                P.wait("sync", "ev_v", casts[u])
                rec_dma(P, "sync", x1_scr[hch], xp[:, hch, :], "st_c")
            P.wait("tensor", "ev_v", casts[3])
            for u in range(4):
                cp = 4 * jc + u
                r_ = cp % 4
                for tp in range(2):
                    bb = P.alloc(1)[0]
                    for s_ in range(4):
                        inc = ("mm", 1) if s_ == 3 else None
                        P.emit("tensor", lambda E, bb=bb, r_=r_, s_=s_, tp=tp:
                               E.matmul(pb[bb][:, 0:512], permt[:, 4 * r_ + s_, :],
                                        x1bf[:, s_, 512 * tp:512 * (tp + 1)],
                                        start=(s_ == 0), stop=(s_ == 3),
                                        skip_group_check=True), inc=inc)
                    mmv = P.val("mm")
                    P.release([bb])
                    P.wait("scalar", "mm", mmv)
                    P.emit("scalar", lambda E, bb=bb, cp=cp, tp=tp: E.copy(
                        out=x1p[:, cp, 512 * tp:512 * (tp + 1)],
                        in_=pb[bb][:, 0:512]), inc=("ev_s", 1))
                    P.bank_free[bb] = ("ev_s", P.val("ev_s"))
            if jc < 3:
                P.wait("vector", "mm")
        P.wait("sync", "st_c")
        P.flush(nc)


def _ph4(nc, P, pb, wg_scr, x1p, h_scr, zbias):
    with ExitStack() as c2:
        ee = c2.enter_context
        gsb = [ee(nc.sbuf_tensor(f"gsb{i}", [128, 16, 512], BF16)) for i in range(2)]
        usb = [ee(nc.sbuf_tensor(f"usb{i}", [128, 16, 512], BF16)) for i in range(2)]
        gsil = [ee(nc.sbuf_tensor(f"gsil{i}", [128, 512], BF16)) for i in range(2)]
        htl = [ee(nc.sbuf_tensor(f"htl{i}", [128, 1024], BF16)) for i in range(2)]

        def load_slabs(jcp):
            rec_dma(P, "sync", gsb[jcp % 2][:, :, :],
                    wg_scr[jcp].transpose([1, 0, 2]), "ld_b")
            return rec_dma(P, "sync", usb[jcp % 2][:, :, :],
                           wg_scr[11 + jcp].transpose([1, 0, 2]), "ld_b")

        slab_ld = {jcp: load_slabs(jcp) for jcp in range(2)}
        base_std = P.val("st_d")
        sil_done = {}
        tt_done = {}
        for jcp in range(11):
            P.wait("tensor", "ld_b", slab_ld[jcp])
            for rp in range(4):
                hpar = rp % 2
                for tp in range(2):
                    gu_it = 8 * jcp + 2 * rp + tp
                    gb = P.alloc(4)
                    rec_quad(P, pb, gb,
                             lambda k, jcp=jcp, rp=rp: gsb[jcp % 2][:, k, rp:512:4],
                             lambda k, sl, tp=tp: x1p[:, k, 512 * tp + 128 * sl:512 * tp + 128 * sl + 128],
                             16)
                    ub = P.alloc(4)
                    rec_quad(P, pb, ub,
                             lambda k, jcp=jcp, rp=rp: usb[jcp % 2][:, k, rp:512:4],
                             lambda k, sl, tp=tp: x1p[:, k, 512 * tp + 128 * sl:512 * tp + 128 * sl + 128],
                             16)
                    P.release(gb)
                    P.release(ub)
                    P.wait("scalar", "mm", max(P.bank_prod[b][1] for b in gb))
                    spar = gu_it % 2
                    if gu_it >= 2:
                        P.wait("scalar", "ev_v", tt_done[gu_it - 2])
                    for i, b in enumerate(gb):
                        inc = ("ev_s", 1) if i == 3 else None
                        P.emit("scalar", lambda E, b=b, spar=spar, i=i: E.activation(
                            out=gsil[spar][:, 128 * i:128 * (i + 1)],
                            in_=pb[b][:, 0:128], func=AF.Silu, bias=zbias[:, :]),
                            inc=inc)
                    sil_done[gu_it] = P.val("ev_s")
                    for b in gb:
                        P.bank_free[b] = ("ev_s", sil_done[gu_it])
                    P.wait("vector", "mm", max(P.bank_prod[b][1] for b in ub))
                    P.wait("vector", "ev_s", sil_done[gu_it])
                    if 4 * jcp + rp >= 2:
                        P.wait("vector", "st_d",
                               base_std + 64 * ((4 * jcp + rp) - 1))
                    for i, b in enumerate(ub):
                        inc = ("ev_v", 1) if i == 3 else None
                        P.emit("vector", lambda E, b=b, spar=spar, hpar=hpar, tp=tp, i=i:
                               E.tensor_tensor(
                                   out=htl[hpar][:, 512 * tp + 128 * i:512 * tp + 128 * (i + 1)],
                                   in0=pb[b][:, 0:128],
                                   in1=gsil[spar][:, 128 * i:128 * (i + 1)],
                                   op=OP.mult), inc=inc)
                    tt_done[gu_it] = P.val("ev_v")
                    for b in ub:
                        P.bank_free[b] = ("ev_v", tt_done[gu_it])
                P.wait("sync", "ev_v", tt_done[8 * jcp + 2 * rp + 1])
                rec_dma(P, "sync", h_scr[4 * jcp + rp], htl[hpar][:, :], "st_d",
                        n_inc=64)
            if jcp + 2 <= 10:
                P.wait("sync", "mm")
                slab_ld[jcp + 2] = load_slabs(jcp + 2)
        P.wait("sync", "st_d")
        P.flush(nc)


def _ph5(nc, P, pb, wd_scr, h_scr, x1_scr, rows_scr, onesd, out_d,
         zbias, bufA, bufB, bufC):
    y = bufA
    bcb = bufC.reshape([128, 16384])
    bcf = bufC.bitcast(F32).reshape([128, 8192])
    bbf = bufB.bitcast(F32).reshape([128, 8192])
    ysq = [bcb[:, 0:1024], bcb[:, 1024:2048]]
    zbf = [bcb[:, 2048:3072], bcb[:, 3072:4096]]
    x1c = [bcf[:, 2048:3072], bcf[:, 3072:4096]]
    srows = bcf[:, 4096:8192]
    rrbc = bbf[:, 0:1024]
    m2bc = bbf[:, 1024:2048]
    r2bc = bbf[:, 2048:3072]
    outc = [bbf[:, 3072:4096], bbf[:, 4096:5120]]
    with ExitStack() as c2:
        ee = c2.enter_context
        wdq = [ee(nc.sbuf_tensor(f"wdq{i}", [128, 11, 512], BF16))
               for i in range(4)]
        hsb2 = [bufB.reshape([128, 32, 512])[:, 0:22, :],
                bufC.reshape([128, 32, 512])[:, 0:22, :]]

        def hslice(k, sl):
            if k < 22:
                return hsb2[0][:, k, 128 * sl:128 * (sl + 1)]
            return hsb2[1][:, k - 22, 128 * sl:128 * (sl + 1)]

        order = [(th, og, ei) for th in range(2) for og in range(4)
                 for ei in range(4)]
        q_done = {}
        q_ld = {}

        def load_q(qi):
            th, og, ei = order[qi]
            q_ld[qi] = rec_dma(P, "sync", wdq[qi % 4][:, :, :],
                               wd_scr[og, 11 * ei:11 * (ei + 1)]
                               .transpose([1, 0, 2]), "ld_b")

        def load_h(th):
            rec_dma(P, "sync", hsb2[0][:, :, :],
                    h_scr[0:22, :, 512 * th:512 * (th + 1)].transpose([1, 0, 2]),
                    "ld_d")
            return rec_dma(P, "sync", hsb2[1][:, :, :],
                           h_scr[22:44, :, 512 * th:512 * (th + 1)]
                           .transpose([1, 0, 2]), "ld_d")

        h_ld = {0: load_h(0)}
        load_q(0)
        load_q(1)
        th_last_mm = {}
        for qi, (th, og, ei) in enumerate(order):
            if th == 1 and og == 0 and ei == 0:
                P.wait("sync", "mm", th_last_mm[0])
                h_ld[1] = load_h(1)
            if qi + 2 < len(order):
                if qi >= 2:
                    P.wait("sync", "mm", q_done.get(qi - 2, 0))
                load_q(qi + 2)
            P.wait("tensor", "ld_d", h_ld[th])
            P.wait("tensor", "ld_b", q_ld[qi])
            kc0 = 11 * (order[qi][2])
            for oc in range(4):
                och = 4 * og + oc
                bset = P.alloc(4)
                rec_quad(P, pb, bset,
                         lambda k, qi=qi, oc=oc: wdq[qi % 4][:, k - 11 * order[qi][2], 128 * oc:128 * (oc + 1)],
                         lambda k, sl: hslice(k, sl),
                         11, k0=11 * ei)
                P.release(bset)
                if ei == 0:
                    P.wait("scalar", "mm", max(P.bank_prod[b][1] for b in bset))
                    for i, b in enumerate(bset):
                        inc = ("ev_s", 1) if i == 3 else None
                        P.emit("scalar", lambda E, b=b, och=och, th=th, i=i:
                               E.copy(out=y[:, och, 512 * th + 128 * i:512 * th + 128 * (i + 1)],
                                      in_=pb[b][:, 0:128]), inc=inc)
                    evs = P.val("ev_s")
                    for b in bset:
                        P.bank_free[b] = ("ev_s", evs)
                else:
                    P.wait("vector", "mm", max(P.bank_prod[b][1] for b in bset))
                    if ei == 1:
                        P.wait("vector", "ev_s")
                    for i, b in enumerate(bset):
                        inc = ("ev_v", 1) if i == 3 else None
                        P.emit("vector", lambda E, b=b, och=och, th=th, i=i:
                               E.tensor_tensor(
                                   out=y[:, och, 512 * th + 128 * i:512 * th + 128 * (i + 1)],
                                   in0=y[:, och, 512 * th + 128 * i:512 * th + 128 * (i + 1)],
                                   in1=pb[b][:, 0:128], op=OP.add),
                               inc=inc)
                    evv = P.val("ev_v")
                    for b in bset:
                        P.bank_free[b] = ("ev_v", evv)
            q_done[qi] = P.val("mm")
            if th == 0 and og == 3 and ei == 3:
                th_last_mm[0] = P.val("mm")
        import os as _os
        tail5 = int(_os.environ.get("TAIL5", "9"))
        if tail5 == 0:
            P.flush(nc)
            return
        _stats_rows(nc, P, pb, y, None, ysq, onesd, srows, zbias, False)
        _rows_rms(nc, P, srows, rows_scr, zbias, 18, RMS_EPS, 2048.0)
        if tail5 == 1:
            P.flush(nc)
            return
        ldx = rec_dma(P, "sync", rrbc,
                      rows_scr[18:19, :].partition_broadcast(128), "ld_f")
        P.wait("vector", "ld_f", ldx)
        P.wait("gpsimd", "mm", q_done[len(order) - 1])
        x1_ld = {hch: rec_dma(P, "gpsimd", x1c[hch], x1_scr[hch], "ld_g")
                 for hch in range(2)}
        res_done = {}
        for hch in range(16):
            P.wait("vector", "ld_g", x1_ld[hch])
            P.emit("vector", lambda E, hch=hch: E.tensor_tensor(
                out=y[:, hch, :], in0=y[:, hch, :], in1=rrbc, op=OP.mult))
            P.emit("vector", lambda E, hch=hch: E.tensor_tensor(
                out=y[:, hch, :], in0=y[:, hch, :], in1=x1c[hch % 2], op=OP.add),
                inc=("ev_v", 1))
            res_done[hch] = P.val("ev_v")
            if hch + 2 <= 15:
                P.wait("gpsimd", "ev_v", res_done[hch])
                x1_ld[hch + 2] = rec_dma(P, "gpsimd", x1c[hch % 2],
                                         x1_scr[hch + 2], "ld_g")
        if tail5 == 2:
            P.flush(nc)
            return
        _stats_rows(nc, P, pb, y, zbf, ysq, onesd, srows, zbias, True)
        _rows_finish(nc, P, srows, rows_scr, zbias, 20, 21, LN_EPS, 2048.0)
        if tail5 == 3:
            P.flush(nc)
            return
        rec_dma(P, "sync", m2bc, rows_scr[20:21, :].partition_broadcast(128), "ld_f")
        ldr = rec_dma(P, "sync", r2bc, rows_scr[21:22, :].partition_broadcast(128),
                      "ld_f")
        P.wait("vector", "ld_f", ldr)
        for hch in range(16):
            if hch >= 2:
                P.wait("vector", "st_out", 16 * (hch - 1))
            P.emit("vector", lambda E, hch=hch: E.tensor_tensor(
                out=y[:, hch, :], in0=y[:, hch, :], in1=m2bc, op=OP.subtract))
            P.emit("vector", lambda E, hch=hch: E.tensor_tensor(
                out=outc[hch % 2], in0=y[:, hch, :], in1=r2bc, op=OP.mult),
                inc=("ev_v", 1))
            P.wait("sync", "ev_v")
            rec_dma(P, "sync", out_d[hch], outc[hch % 2], "st_out")
        P.wait("sync", "st_out")
        P.flush(nc)



def _host_inputs(inputs):
    x = np.ascontiguousarray(np.asarray(inputs["L0_attn_hidden_states"],
                                        dtype=np.float32))
    w_qkv = np.asarray(inputs["L0_attn_w_qkv"], dtype=np.float32)
    w_o = np.asarray(inputs["L0_attn_w_o"], dtype=np.float32)
    gupk = np.ascontiguousarray(np.asarray(inputs["L0_ffn_gate_up_packed"],
                                           dtype=np.int32))
    gusc = np.ascontiguousarray(np.asarray(inputs["L0_ffn_gate_up_scales"],
                                           dtype=np.float16))
    dnpk = np.ascontiguousarray(np.asarray(inputs["L0_ffn_down_packed"],
                                           dtype=np.int32))
    dnsc = np.ascontiguousarray(np.asarray(inputs["L0_ffn_down_scales"],
                                           dtype=np.float16))

    wqkvT = np.ascontiguousarray(w_qkv.T.reshape(H, 12, 512).transpose(1, 0, 2))
    woT = np.ascontiguousarray(w_o.T.reshape(H, 4, 512).transpose(1, 0, 2))
    gupv = gupk.view(np.float16)
    dnpv = dnpk.view(np.float16)

    tri = np.triu(np.ones((128, 128), dtype=np.float32)).astype(ml_dtypes.bfloat16)
    onesm = np.ones((128, 128), dtype=ml_dtypes.bfloat16)
    zerom = np.zeros((128, 128), dtype=ml_dtypes.bfloat16)

    perms = np.zeros((128, 16, 128), dtype=ml_dtypes.bfloat16)
    for r in range(4):
        for s_ in range(4):
            for m in range(32 * s_, 32 * s_ + 32):
                perms[4 * m + r - 128 * s_, 4 * r + s_, m] = 1.0

    in_maps = []
    for core in range(8):
        b, cid = core // 2, core % 2
        idx = _own_token_index(cid)
        xTb = np.ascontiguousarray(x[b].T)
        in_maps.append({
            "xT_full": xTb,
            "xT_own": np.ascontiguousarray(xTb[:, idx]),
            "wqkvT": wqkvT, "woT": woT,
            "gup": gupv, "gus": gusc, "dnp": dnpv, "dns": dnsc,
            "mask_ev": np.ascontiguousarray(tri if cid == 0 else onesm),
            "mask_od": np.ascontiguousarray(zerom if cid == 0 else tri),
            "permsel": perms,
        })
    return in_maps


def kernel(**inputs):
    in_maps = _host_inputs(inputs)
    nc, _ = build_nc()
    res = run_bass_kernel_spmd(nc, in_maps, core_ids=list(range(8)))
    out = np.zeros((B, S, H), dtype=np.float32)
    for core in range(8):
        b, cid = core // 2, core % 2
        idx = _own_token_index(cid)
        oc = np.asarray(res.results[core]["out"]).reshape(H, T)
        out[b, idx, :] = oc.T
    return out
